# version 01
# baseline (speedup 1.0000x reference)
import numpy as np
import ml_dtypes
import concourse.bass as bass
import concourse.mybir as mybir
from concourse.bass_utils import run_bass_kernel_spmd
from contextlib import ExitStack

F32 = mybir.dt.float32
BF16 = mybir.dt.bfloat16
I32 = mybir.dt.int32
AF = mybir.ActivationFunctionType
ALU = mybir.AluOpType
AX = mybir.AxisListType
ts = bass.ts

B_, S, D = 4, 4096, 4096
DH = 1024
FQKV = 3 * DH
I_FFN = 11008
GROUP = 128
NEXP, RANK = 28, 8
ER = NEXP * RANK
OWN = S // 2
NCORE = 8
P = 128
CT = D // P
DT = DH // P
ITI = I_FFN // P
IHALF = I_FFN // 2
CHALF = D // 2
NG_GU = D // GROUP
NG_D = I_FFN // GROUP
SC2 = 2.0


class Bld:

    def __init__(self, nc, es):
        self.nc = nc
        self.es = es
        self.sems = {}
        self.cnt = {}
        self.seen = {}
        self.pumps = []
        self.rr = {}

    def sem(self, name):
        if name not in self.sems:
            self.sems[name] = self.es.enter_context(self.nc.semaphore("tk_" + name))
            self.cnt[name] = 0
        return self.sems[name]

    def sig(self, name, inst, k=1):
        inst.then_inc(self.sem(name), k)
        self.cnt[name] += k
        return (name, self.cnt[name])

    def at(self, name):
        self.sem(name)
        return (name, self.cnt[name])

    def wait(self, eng, ename, *tickets):
        for t in tickets:
            if t is None:
                continue
            name, v = t
            if v <= 0:
                continue
            key = (ename, name)
            if self.seen.get(key, -1) >= v:
                continue
            eng.wait_ge(self.sem(name), v)
            self.seen[key] = v

    NDMA = 6

    def dma(self, ename, eng, call, k=16):
        i = self.rr.get(ename, 0)
        self.rr[ename] = i + 1
        lane = f"{ename}{i % self.NDMA}"
        self.wait(eng, ename, self.at(lane))
        return self.sig(lane, call(), k)

    def fence(self, engs):
        marks = [self.at(n) for n in list(self.cnt)]
        for eng, ename in engs:
            for m in marks:
                self.wait(eng, ename, m)

    def pump(self, n=1):
        for _ in range(n):
            for g in list(self.pumps):
                try:
                    next(g)
                except StopIteration:
                    self.pumps.remove(g)

    def switched(self, engines, fn):
        cnt0, seen0, rr0 = dict(self.cnt), dict(self.seen), dict(self.rr)
        final = None
        for arm in self.nc.switch_core_id(engines=engines, n=128):
            self.cnt, self.seen, self.rr = dict(cnt0), dict(seen0), dict(rr0)
            fn(arm.logical % 2)
            if final is None:
                final = (dict(self.cnt), dict(self.seen), dict(self.rr))
            else:
                assert self.cnt == final[0], "switched bodies emitted differing counts"
        self.cnt, self.seen, self.rr = final


def _dyn_rows(ap, ro, i, w=P):
    if isinstance(ro, int):
        return ap[(ro + i) * P:(ro + i) * P + w, :]
    return ap[ts(ro[i], P), :]


def _dyn_cols(ap, ro, i):
    if isinstance(ro, int):
        return ap[:, (ro + i) * P:(ro + i + 1) * P]
    return ap[:, ts(ro[i], P)]



def emit_consts(b, ident_in):
    nc, es = b.nc, b.es
    cons = {}
    cons["ident"] = es.enter_context(nc.sbuf_tensor("c_ident", [P, P], BF16))
    cons["eps"] = es.enter_context(nc.sbuf_tensor("c_eps", [P, 1], F32))
    cons["ones_b"] = es.enter_context(nc.sbuf_tensor("c_onesb", [1, P], BF16))
    cons["oh"] = es.enter_context(nc.sbuf_tensor("c_oh", [P, 2 * ITI], BF16))
    gp = nc.gpsimd
    cons["t_ident"] = b.dma("ds", nc.sync, lambda: nc.sync.dma_start(out=cons["ident"][:], in_=ident_in))
    gp.memset(cons["eps"][:], 1e-5)
    gp.memset(cons["ones_b"][:], 1.0)
    t0 = b.sig("g", gp.memset(cons["oh"][:], 0.0))
    b.wait(gp, "g", t0)
    cons["t_g"] = b.sig("g", gp.memset(cons["oh"][:, ITI:ITI + 1], 1.0))
    return cons


def oh_lhsT(cons, n_g, hot):
    return cons["oh"][:, ITI - hot:ITI - hot + n_g]


def mk_ln(b, tag, stack=None, sink=True):
    nc = b.nc
    es = stack if stack is not None else b.es
    Sb = lambda n, s, d: es.enter_context(nc.sbuf_tensor(tag + n, s, d))
    d_ = {
        "ssum": Sb("_ssum", [P, 1], F32), "ssq": Sb("_ssq", [P, 1], F32),
        "mu": Sb("_mu", [P, 1], F32), "var": Sb("_var", [P, 1], F32),
        "sd": Sb("_sd", [P, 1], F32), "rstd": Sb("_rstd", [P, 1], F32),
    }
    if sink:
        d_["sink"] = Sb("_sink", [P, D], F32)
    return d_


def emit_ln_tile(b, ln, x_sb, t_x, out_sb, war):
    nc = b.nc
    v, a = nc.vector, nc.scalar
    sink = ln.get("sink")
    sink_ap = sink[:] if sink is not None else out_sb[:]
    b.wait(v, "v", t_x, b.at("v"))
    b.sig("v", v.reduce_sum(ln["ssum"][:], x_sb[:], axis=AX.X))
    b.wait(a, "a", t_x, b.at("a"))
    if sink is None:
        b.wait(a, "a", war)
    tq = b.sig("a", a.activation(sink_ap, x_sb[:], AF.Square, accum_out=ln["ssq"][:]))
    b.wait(v, "v", tq, b.at("v"))
    b.sig("v", v.tensor_scalar_mul(ln["mu"][:], ln["ssum"][:], 1.0 / D))
    b.wait(v, "v", b.at("v"))
    b.sig("v", v.scalar_tensor_tensor(ln["var"][:], ln["mu"][:], -1.0, ln["mu"][:], ALU.mult, ALU.mult))
    b.wait(v, "v", b.at("v"))
    tv = b.sig("v", v.scalar_tensor_tensor(ln["var"][:], ln["ssq"][:], 1.0 / D, ln["var"][:], ALU.mult, ALU.add))
    b.wait(a, "a", tv, cons_tg(b))
    tsd = b.sig("a", a.activation(ln["sd"][:], ln["var"][:], AF.Sqrt, bias=ln["eps"][:]))
    b.wait(v, "v", tsd, b.at("v"))
    b.sig("v", v.reciprocal(ln["rstd"][:], ln["sd"][:]))
    b.wait(v, "v", b.at("v"))
    b.sig("v", v.tensor_scalar_mul(ln["mu"][:], ln["mu"][:], -1.0))
    b.wait(v, "v", war, b.at("v"))
    return b.sig("v", v.tensor_scalar(out_sb[:], x_sb[:], ln["mu"][:], ln["rstd"][:], ALU.add, ALU.mult))


_CONS_TG = {}


def cons_tg(b):
    return _CONS_TG.get(id(b))



def emit_attn(b, tag, aps, cons, scr):
    nc, es = b.nc, b.es
    sy, gp, v, a, t = nc.sync, nc.gpsimd, nc.vector, nc.scalar, nc.tensor
    ident = cons["ident"]
    ones_b = cons["ones_b"]
    MCH = 1024
    NCH = S // MCH
    SUB = MCH // P
    NFT = FQKV // P

    with ExitStack() as ph:
        Sb = lambda n, s, d: ph.enter_context(nc.sbuf_tensor(tag + n, s, d))
        Ps = lambda n, s, d: ph.enter_context(nc.psum_tensor(tag + n, s, d))
        xd = [Sb(f"_x{i}", [P, D], F32) for i in range(2)]
        lnb = [Sb(f"_lnb{i}", [P, D], BF16) for i in range(4)]
        lnT = Sb("_lnT", [P, CT, MCH], BF16)
        wq = [Sb(f"_wq{i}", [P, CT, P], BF16) for i in range(2)]
        ev = [Sb(f"_ev{i}", [P, 512], BF16) for i in range(4)]
        vt = [Sb(f"_vt{i}", [P, 4, P], BF16) for i in range(2)]
        ln = mk_ln(b, tag + "_ln", ph)
        ln["eps"] = cons["eps"]
        ps_tr = [Ps(f"_ptr{i}", [P, 512], BF16) for i in range(2)]
        ps_qkv = [Ps(f"_pq{i}", [P, 512], F32) for i in range(4)]
        ps_vt = [Ps(f"_pvt{i}", [P, 512], BF16) for i in range(2)]

        t_x = {}
        t_apply = {}
        t_trev = {}
        t_ev = {}
        t_evd = {}
        t_vt = {}
        t_vtd = {}
        t_wq = {}
        t_wqu = {}
        t_qkv_last = {}

        WPT = 8
        NW = CT // 4

        for ch in range(NCH):
            for st in range(SUB):
                gst = ch * SUB + st
                m0 = ch * MCH + st * P
                b.wait(sy, "ds", t_apply.get(gst - 2))
                t_x[gst] = b.dma("ds", sy, lambda: sy.dma_start(
                    out=xd[st % 2][:], in_=aps["xsrc"][m0:m0 + P, :]))
                war = t_trev.get((gst - 4) * NW + NW - 1)
                t_apply[gst] = emit_ln_tile(b, ln, xd[st % 2], t_x[gst], lnb[st % 4], war)
                for w in range(NW):
                    gw = gst * NW + w
                    b.wait(t, "p", t_apply[gst], t_trev.get(gw - 2), cons["t_ident"],
                           t_qkv_last.get(ch - 1))
                    for j in range(4):
                        ct = w * 4 + j
                        inst = t.transpose(ps_tr[gw % 2][:, j * P:(j + 1) * P],
                                           lnb[st % 4][:, ct * P:(ct + 1) * P], ident[:])
                    tp = b.sig("p", inst)
                    b.wait(v, "v", tp, b.at("v"))
                    t_trev[gw] = b.sig("v", v.tensor_copy(
                        lnT[:, w * 4:(w + 1) * 4, st * P:(st + 1) * P],
                        ps_tr[gw % 2][:].rearrange("p (j m) -> p j m", j=4)))
            t_lnT = t_trev[(ch * SUB + SUB - 1) * NW + NW - 1]
            for ft in range(NFT):
                gft = ch * NFT + ft
                slot = gft % 2
                b.wait(gp, "dg", t_wqu.get(gft - 2))
                t_wq[gft] = b.dma("dg", gp, lambda: gp.dma_start(
                    out=wq[slot][:],
                    in_=aps["wqkv"].rearrange("(ct p) f -> p ct f", p=P)[:, :, ft * P:(ft + 1) * P]))
                for hh in range(2):
                    gph = 2 * gft + hh
                    pslot = gph % 4
                    b.wait(t, "p", t_wq[gft], t_lnT, t_ev.get(gph - 4))
                    for ct in range(CT):
                        inst = t.matmul(ps_qkv[pslot][:], wq[slot][:, ct, :],
                                        lnT[:, ct, hh * 512:(hh + 1) * 512],
                                        start=(ct == 0), stop=(ct == CT - 1))
                    tmm = b.sig("p", inst)
                    if ft == NFT - 1 and hh == 1:
                        t_qkv_last[ch] = tmm
                    b.wait(a, "a", tmm, t_evd.get(gph - 4), b.at("a"))
                    tev = b.sig("a", a.copy(ev[pslot][:], ps_qkv[pslot][:]))
                    t_ev[gph] = tev
                    mc0 = ch * MCH + hh * 512
                    if ft < 8:
                        b.wait(sy, "ds", tev)
                        t_evd[gph] = b.dma("ds", sy, lambda: sy.dma_start(
                            out=scr["qT"][ft * P:(ft + 1) * P, mc0:mc0 + 512], in_=ev[pslot][:]))
                    elif ft < 16:
                        fk = ft - 8
                        b.wait(sy, "ds", tev)
                        t_evd[gph] = b.dma("ds", sy, lambda: sy.dma_start(
                            out=scr["kT"][fk * P:(fk + 1) * P, mc0:mc0 + 512], in_=ev[pslot][:]))
                    else:
                        fv = ft - 16
                        b.wait(t, "p", tev, t_vt.get(gph - 2))
                        for j in range(4):
                            inst = t.transpose(ps_vt[gph % 2][:, j * P:(j + 1) * P],
                                               ev[pslot][:, j * P:(j + 1) * P], ident[:])
                        tvp = b.sig("p", inst)
                        b.wait(v, "v", tvp, t_vtd.get(gph - 2), b.at("v"))
                        t_vt[gph] = b.sig("v", v.tensor_copy(
                            vt[gph % 2][:], ps_vt[gph % 2][:].rearrange("p (j m) -> p j m", j=4)))
                        b.wait(sy, "ds", t_vt[gph])
                        t_vtd[gph] = b.dma("ds", sy, lambda: sy.dma_start(
                            out=scr["v"][mc0:mc0 + 512, fv * P:(fv + 1) * P]
                            .rearrange("(j p) d -> p j d", p=P),
                            in_=vt[gph % 2][:]))
                        t_evd[gph] = t_vtd[gph]
                t_wqu[gft] = b.at("p")
                b.pump()

    b.fence([(sy, "ds"), (gp, "dg"), (v, "v"), (a, "a"), (t, "p"), (gp, "g")])

    NQ = OWN // P
    with ExitStack() as ph:
        Sb = lambda n, s, d: ph.enter_context(nc.sbuf_tensor(tag + n, s, d))
        Ps = lambda n, s, d: ph.enter_context(nc.psum_tensor(tag + n, s, d))
        kT_sb = Sb("_kTs", [P, DT, S], BF16)
        v_sb = Sb("_vs", [P, S // P, DH], BF16)
        qT_all = Sb("_qTall", [P, DT, OWN], BF16)
        bqb = Sb("_bqb", [P, DT], BF16)
        bqk = Sb("_bqk", [1, S], BF16)
        scm = Sb("_sc", [P, S], F32)
        att = Sb("_att", [P, S], BF16)
        atT = Sb("_atT", [P, S // P, P], BF16)
        nmax = Sb("_nmax", [P, 1], F32)
        rsum = Sb("_rsum", [P, 1], F32)
        rr = [Sb(f"_rr{i}", [P, 1], F32) for i in range(2)]
        ao = [Sb(f"_ao{i}", [P, DH], BF16) for i in range(2)]
        ps_s = [Ps(f"_pss{i}", [P, 512], F32) for i in range(2)]
        ps_at = [Ps(f"_psat{i}", [P, 512], BF16) for i in range(2)]
        ps_o = [Ps(f"_pso{i}", [P, 512], F32) for i in range(2)]

        tk = b.dma("ds", sy, lambda: sy.dma_start(
            out=kT_sb[:], in_=scr["kT"].rearrange("(g p) n -> p g n", p=P)))
        tv_ = b.dma("ds", sy, lambda: sy.dma_start(
            out=v_sb[:], in_=scr["v"].rearrange("(g p) n -> p g n", p=P)))
        tq_all = [None]

        def _ld_q(half):
            tq_all[0] = b.dma("ds", sy, lambda: sy.dma_start(
                out=qT_all[:],
                in_=scr["qT"][:, half * OWN:(half + 1) * OWN]
                .rearrange("(g p) q -> p g q", p=P)))

        b.switched([sy], _ld_q)
        tbq = b.dma("dg", gp, lambda: gp.dma_start(out=bqb[:], in_=aps["bqcol"]))
        t_bqk = {}
        for kc in range(8):
            b.wait(t, "p", tk, tbq, t_bqk.get(kc - 2))
            for d in range(DT):
                inst = t.matmul(ps_s[kc % 2][0:1, :], bqb[:, d:d + 1],
                                kT_sb[:, d, kc * 512:(kc + 1) * 512],
                                start=(d == 0), stop=(d == DT - 1))
            tm = b.sig("p", inst)
            b.wait(v, "v", tm, b.at("v"))
            t_bqk[kc] = b.sig("v", v.tensor_copy(bqk[:, kc * 512:(kc + 1) * 512],
                                                 ps_s[kc % 2][0:1, :]))
        t_bqk_all = b.at("v")

        t_q = {}
        t_sc = {}
        t_tex = {}
        t_soft = {}
        t_atT = {}
        t_atTl = {}
        t_av = {}
        t_ao = {}
        t_aod = {}

        def scores(i):
            for kc in range(8):
                pslot = kc % 2
                b.wait(t, "p", tq_all[0], tk, t_bqk_all, t_sc.get((i, kc - 2)),
                       t_sc.get((i - 1, kc + 6)))
                for d in range(DT):
                    t.matmul(ps_s[pslot][:], qT_all[:, d, i * P:(i + 1) * P],
                             kT_sb[:, d, kc * 512:(kc + 1) * 512],
                             start=(d == 0), stop=False)
                inst = t.matmul(ps_s[pslot][:], ones_b[:], bqk[:, kc * 512:(kc + 1) * 512],
                                start=False, stop=True)
                tm = b.sig("p", inst)
                b.wait(a, "a", tm, t_tex.get(i - 1), b.at("a"))
                t_sc[(i, kc)] = b.sig("a", a.mul(scm[:, kc * 512:(kc + 1) * 512],
                                                 ps_s[pslot][:], 0.03125))

        def soft1(i):
            b.wait(v, "v", b.at("v"), *[t_sc[(i, kc)] for kc in range(8)])
            tmx = b.sig("v", v.reduce_max(nmax[:], scm[:], axis=AX.X, negate=True))
            b.wait(a, "a", tmx, t_atTl.get(i - 1), b.at("a"))
            t_tex[i] = b.sig("a", a.activation(att[:], scm[:], AF.Exp,
                                               bias=nmax[:], accum_out=rsum[:]))

        def soft2(i):
            ip = i % 2
            tex = t_tex[i]
            b.wait(v, "v", tex, t_ao.get((i - 2, 0)), t_ao.get((i - 2, 1)), b.at("v"))
            trr = b.sig("v", v.reciprocal(rr[ip][:], rsum[:]))
            t_soft[i] = trr
            for w in range(8):
                gw = i * 8 + w
                b.wait(t, "p", tex, t_atT.get(gw - 2))
                for j in range(4):
                    kt = w * 4 + j
                    inst = t.transpose(ps_at[gw % 2][:, j * P:(j + 1) * P],
                                       att[:, kt * P:(kt + 1) * P], ident[:])
                tw_ = b.sig("p", inst)
                b.wait(v, "v", tw_, t_av.get((i - 1, 0)), t_av.get((i - 1, 1)), b.at("v"))
                t_atT[gw] = b.sig("v", v.tensor_copy(
                    atT[:, w * 4:(w + 1) * 4, :],
                    ps_at[gw % 2][:].rearrange("p (j m) -> p j m", j=4)))
            t_atTl[i] = t_atT[i * 8 + 7]
            for dc in range(2):
                b.wait(t, "p", t_atTl[i], tv_, t_ao.get((i - 1, dc)))
                for kt in range(S // P):
                    inst = t.matmul(ps_o[dc][:], atT[:, kt, :],
                                    v_sb[:, kt, dc * 512:(dc + 1) * 512],
                                    start=(kt == 0), stop=(kt == S // P - 1))
                tm = b.sig("p", inst)
                t_av[(i, dc)] = tm
                b.wait(v, "v", tm, trr, t_aod.get(i - 2), b.at("v"))
                t_ao[(i, dc)] = b.sig("v", v.tensor_scalar_mul(
                    ao[ip][:, dc * 512:(dc + 1) * 512], ps_o[dc][:], rr[ip][:]))
            b.wait(sy, "ds", t_ao[(i, 0)], t_ao[(i, 1)])
            t_aod[i] = b.dma("ds", sy, lambda: sy.dma_start(
                out=scr["ao"][i * P:(i + 1) * P, :], in_=ao[ip][:]))

        scores(0)
        for i in range(NQ):
            soft1(i)
            if i + 1 < NQ:
                scores(i + 1)
            soft2(i)
            b.pump()

    b.fence([(sy, "ds"), (gp, "dg"), (v, "v"), (a, "a"), (t, "p")])

    with ExitStack() as ph:
        Sb = lambda n, s, d: ph.enter_context(nc.sbuf_tensor(tag + n, s, d))
        Ps = lambda n, s, d: ph.enter_context(nc.psum_tensor(tag + n, s, d))
        wo_sb = Sb("_wo", [P, DT, D], BF16)
        bvb = Sb("_bvb", [P, DT], BF16)
        bo_sb = Sb("_bo", [1, D], F32)
        bc_f = Sb("_bcf", [1, D], F32)
        bc_b = Sb("_bcb", [1, D], BF16)
        aoi = [Sb(f"_aoi{i}", [P, DH], BF16) for i in range(2)]
        aoT = [Sb(f"_aoT{i}", [P, DT, P], BF16) for i in range(2)]
        xo = [Sb(f"_xo{i}", [P, D], F32) for i in range(2)]
        xn = [Sb(f"_xn{i}", [P, D], F32) for i in range(2)]
        ps_t3 = [Ps(f"_pt3{i}", [P, 512], BF16) for i in range(2)]
        ps_y = [Ps(f"_py{i}", [P, 512], F32) for i in range(2)]

        two = b.dma("dg", gp, lambda: gp.dma_start(
            out=wo_sb[:], in_=aps["wo"].rearrange("(g p) n -> p g n", p=P)))
        tbv = b.dma("dg", gp, lambda: gp.dma_start(out=bvb[:], in_=aps["bvcol"]))
        tbo = b.dma("ds", sy, lambda: sy.dma_start(out=bo_sb[:], in_=aps["bo"]))
        t_bc = {}
        for cc in range(8):
            b.wait(t, "p", two, tbv, t_bc.get(cc - 2))
            for d in range(DT):
                inst = t.matmul(ps_y[cc % 2][0:1, :], bvb[:, d:d + 1],
                                wo_sb[:, d, cc * 512:(cc + 1) * 512],
                                start=(d == 0), stop=(d == DT - 1))
            tm = b.sig("p", inst)
            b.wait(v, "v", tm, tbo, b.at("v"))
            t_bc[cc] = b.sig("v", v.tensor_copy(bc_f[:, cc * 512:(cc + 1) * 512],
                                                ps_y[cc % 2][0:1, :]))
        b.wait(v, "v", b.at("v"))
        b.sig("v", v.tensor_add(bc_f[:], bc_f[:], bo_sb[:]))
        b.wait(v, "v", b.at("v"))
        tbc = b.sig("v", v.tensor_copy(bc_b[:], bc_f[:]))

        t_aoi = {}
        t_aoT = {}
        t_xo = {}
        t_y = {}
        t_xw = {}
        for i in range(OWN // P):
            ip = i % 2
            b.wait(sy, "ds", t_aoT.get(i - 2))
            t_aoi[i] = b.dma("ds", sy, lambda: sy.dma_start(out=aoi[ip][:],
                                                in_=scr["ao"][i * P:(i + 1) * P, :]))
            b.wait(sy, "ds", t_xw.get(i - 2))
            t_xo[i] = b.dma("ds", sy, lambda: sy.dma_start(out=xo[ip][:], in_=aps["xres"][i * P:(i + 1) * P, :]))
            for w in range(2):
                b.wait(t, "p", t_aoi[i], t_aoT.get(i - 1), cons["t_ident"],
                       t_y.get((i - 1, 6 + w)))
                for j in range(4):
                    jj = w * 4 + j
                    inst = t.transpose(ps_t3[w][:, j * P:(j + 1) * P],
                                       aoi[ip][:, jj * P:(jj + 1) * P], ident[:])
                tp = b.sig("p", inst)
                b.wait(v, "v", tp, b.at("v"))
                t_aoT[i] = b.sig("v", v.tensor_copy(
                    aoT[ip][:, w * 4:(w + 1) * 4, :],
                    ps_t3[w][:].rearrange("p (j m) -> p j m", j=4)))
            for cc in range(8):
                b.wait(t, "p", t_aoT[i], two, tbc, t_y.get((i, cc - 2)),
                       t_y.get((i - 1, cc + 6)))
                for d in range(DT):
                    t.matmul(ps_y[cc % 2][:], aoT[ip][:, d, :],
                             wo_sb[:, d, cc * 512:(cc + 1) * 512],
                             start=(d == 0), stop=False)
                inst = t.matmul(ps_y[cc % 2][:], ones_b[:], bc_b[:, cc * 512:(cc + 1) * 512],
                                start=False, stop=True)
                tm = b.sig("p", inst)
                b.wait(v, "v", tm, t_xo[i], t_xw.get(i - 2), b.at("v"))
                t_y[(i, cc)] = b.sig("v", v.tensor_tensor(
                    xn[ip][:, cc * 512:(cc + 1) * 512], ps_y[cc % 2][:],
                    xo[ip][:, cc * 512:(cc + 1) * 512], ALU.add))
            b.wait(sy, "ds", *[t_y[(i, cc)] for cc in range(8)])
            t_xw[i] = b.dma("ds", sy, lambda: sy.dma_start(out=aps["xdst"](i), in_=xn[ip][:]))
            b.pump()

    b.fence([(sy, "ds"), (gp, "dg"), (v, "v"), (a, "a"), (t, "p")])



def build_attn_program():
    nc = bass.Bass("TRN2", target_bir_lowering=False)
    xsrc = nc.dram_tensor("xsrc", [S, D], F32, kind="ExternalInput").ap()
    xres = nc.dram_tensor("xres", [OWN, D], F32, kind="ExternalInput").ap()
    wqkv = nc.dram_tensor("wqkv", [D, FQKV], F32, kind="ExternalInput").ap()
    bqcol = nc.dram_tensor("bqcol", [P, DT], F32, kind="ExternalInput").ap()
    wo = nc.dram_tensor("wo", [DH, D], F32, kind="ExternalInput").ap()
    bvcol = nc.dram_tensor("bvcol", [P, DT], F32, kind="ExternalInput").ap()
    bo = nc.dram_tensor("bo", [1, D], F32, kind="ExternalInput").ap()
    ident = nc.dram_tensor("ident", [P, P], BF16, kind="ExternalInput").ap()
    xout = nc.dram_tensor("xout", [OWN, D], F32, kind="ExternalOutput").ap()

    kT = nc.dram_tensor("s_kT", [DH, S], BF16, kind="Internal").ap()
    qT = nc.dram_tensor("s_qT", [DH, S], BF16, kind="Internal").ap()
    vd = nc.dram_tensor("s_v", [S, DH], BF16, kind="Internal").ap()
    aod = nc.dram_tensor("s_ao", [OWN, DH], BF16, kind="Internal").ap()

    with ExitStack() as es:
        b = Bld(nc, es)
        es.enter_context(nc.Block())
        cons = emit_consts(b, ident)
        _CONS_TG[id(b)] = cons["t_g"]
        aps = {
            "xsrc": xsrc, "wqkv": wqkv, "bqcol": bqcol, "wo": wo,
            "bvcol": bvcol, "bo": bo,
            "xres": xres,
            "xdst": lambda i: xout[i * P:(i + 1) * P, :],
        }
        scr = {"kT": kT, "qT": qT, "v": vd, "ao": aod}
        emit_attn(b, "a", aps, cons, scr)
    return nc


def stage_attn_inputs(x_full, wqkv, bqkv, wo, bo):
    ident = np.eye(P, dtype=ml_dtypes.bfloat16)
    bqcol = np.ascontiguousarray(np.asarray(bqkv[:DH], np.float32).reshape(DT, P).T)
    bvcol = np.ascontiguousarray(np.asarray(bqkv[2 * DH:], np.float32).reshape(DT, P).T)
    maps = []
    for c in range(NCORE):
        seq = c // 2
        half = c % 2
        maps.append({
            "xsrc": np.ascontiguousarray(x_full[seq]),
            "xres": np.ascontiguousarray(x_full[seq][half * OWN:(half + 1) * OWN]),
            "wqkv": np.ascontiguousarray(wqkv),
            "bqcol": bqcol,
            "wo": np.ascontiguousarray(wo),
            "bvcol": bvcol,
            "bo": np.ascontiguousarray(np.asarray(bo, np.float32).reshape(1, D)),
            "ident": ident,
        })
    return maps





def emit_ln_chunk(b, tag, cons, bufs, xsrc_ap, row0, nsub, lnT, war_qkv):
    nc = b.nc
    sy, v, t = nc.sync, nc.vector, nc.tensor
    ident = cons["ident"]
    NW = CT // 4
    st8 = bufs.setdefault("_state", {"t_apply": {}, "t_trev": {}, "g": 0})
    t_apply, t_trev = st8["t_apply"], st8["t_trev"]
    last = None
    for st in range(nsub):
        gst = st8["g"] + st
        nxd = bufs["nxd"]
        b.wait(sy, "ds", t_apply.get(gst - nxd))
        t_x = b.dma("ds", sy, lambda: sy.dma_start(
            out=bufs["xd"][st % nxd][:], in_=xsrc_ap[row0 + st * P:row0 + (st + 1) * P, :]))
        nb = bufs["nlnb"]
        war = t_trev.get((gst - nb) * NW + NW - 1)
        t_apply[gst] = emit_ln_tile(b, bufs["ln"], bufs["xd"][st % nxd], t_x,
                                    bufs["lnb"][st % nb], war)
        for w in range(NW):
            gw = gst * NW + w
            b.wait(t, "p", t_apply[gst], t_trev.get(gw - 2), cons["t_ident"], war_qkv)
            for j in range(4):
                ct = w * 4 + j
                inst = t.transpose(bufs["ps_tr"][gw % 2][:, j * P:(j + 1) * P],
                                   bufs["lnb"][st % nb][:, ct * P:(ct + 1) * P], ident[:])
            tp = b.sig("p", inst)
            b.wait(v, "v", tp, b.at("v"))
            last = b.sig("v", v.tensor_copy(
                lnT[:, w * 4:(w + 1) * 4, st * P:(st + 1) * P],
                bufs["ps_tr"][gw % 2][:].rearrange("p (j m) -> p j m", j=4)))
            t_trev[gw] = last
    st8["g"] += nsub
    return last


def mk_ln_bufs(b, tag, ph, cons, nlnb=4, sink=True, nxd=2):
    nc = b.nc
    Sb = lambda n, s, d: ph.enter_context(nc.sbuf_tensor(tag + n, s, d))
    Ps = lambda n, s, d: ph.enter_context(nc.psum_tensor(tag + n, s, d))
    ln = mk_ln(b, tag + "_ln", ph, sink=sink)
    ln["eps"] = cons["eps"]
    return {
        "nlnb": nlnb,
        "nxd": nxd,
        "xd": [Sb(f"_x{i}", [P, D], F32) for i in range(nxd)],
        "lnb": [Sb(f"_lnb{i}", [P, D], BF16) for i in range(nlnb)],
        "ln": ln,
        "ps_tr": [Ps(f"_ptr{i}", [P, 512], BF16) for i in range(2)],
    }



def emit_ffn1(b, tag, aps, cons, scr):
    nc, es = b.nc, b.es
    sy, gp, v, a, t = nc.sync, nc.gpsimd, nc.vector, nc.scalar, nc.tensor
    ident = cons["ident"]
    MT = 1024
    NMT = OWN // MT
    NMS = MT // P

    with ExitStack() as ph:
        Sb = lambda n, s, d: ph.enter_context(nc.sbuf_tensor(tag + n, s, d))
        Ps = lambda n, s, d: ph.enter_context(nc.psum_tensor(tag + n, s, d))
        lnT = Sb("_lnT", [P, CT, MT], BF16)
        lbufs = mk_ln_bufs(b, tag, ph, cons, nlnb=2)
        at_sb = Sb("_at", [P, CT, ER], BF16)
        gw_sb = Sb("_gw", [P, CT, NEXP], BF16)
        bb1 = Sb("_bb1", [P, D], BF16)
        bb2 = Sb("_bb2", [P, D], BF16)
        bw = [Sb(f"_bw{i}", [P, CT, 256], BF16) for i in range(2)]
        lgt = Sb("_lgt", [P, NEXP], F32)
        v1 = Sb("_v1", [P, 1], F32)
        v2 = Sb("_v2", [P, 1], F32)
        dv = Sb("_dv", [P, 1], F32)
        p1 = Sb("_p1", [P, 1], F32)
        p2 = Sb("_p2", [P, 1], F32)
        m1 = Sb("_m1", [P, NEXP], F32)
        m2 = Sb("_m2", [P, NEXP], F32)
        msk = Sb("_msk", [P, NEXP], F32)
        gsc = Sb("_gsc", [P, NEXP], F32)
        gall = Sb("_gall", [P, NMS, NEXP], F32)
        h1 = Sb("_h1", [P, ER], BF16)
        hT1 = Sb("_hT1", [P, NMS, P], BF16)
        hT2 = Sb("_hT2", [P, NMS, P], BF16)
        xi = [Sb(f"_xi{i}", [P, 256], F32) for i in range(2)]
        xn = [Sb(f"_xn{i}", [P, 256], F32) for i in range(2)]
        ps_lg = Ps("_plg", [P, 512], F32)
        ps_h = Ps("_ph", [P, 512], F32)
        ps_ht = [Ps(f"_pht{i}", [P, 512], BF16) for i in range(2)]
        ps_y = [Ps(f"_py{i}", [P, 512], F32) for i in range(2)]

        tat = b.dma("dg", gp, lambda: gp.dma_start(
            out=at_sb[:], in_=aps["aallT"].rearrange("(ct p) e -> p ct e", p=P)))
        tgw = b.dma("dg", gp, lambda: gp.dma_start(
            out=gw_sb[:], in_=aps["gwT"].rearrange("(ct p) e -> p ct e", p=P)))
        tb1 = b.dma("dg", gp, lambda: gp.dma_start(out=bb1[:], in_=aps["ballT"][0:P, :]))
        tb2 = b.dma("dg", gp, lambda: gp.dma_start(
            out=bb2[0:ER - P, :], in_=aps["ballT"][P:ER, :]))

        t_bw = {}
        t_bwu = {}
        t_y = {}
        t_xi = {}
        t_xw = {}
        for mt in range(NMT):
            t_lnT = emit_ln_chunk(b, tag, cons, lbufs, aps["xsrc"], mt * MT, NMS, lnT,
                                  b.at("p"))
            t_hT = {}
            t_gv = {}
            for ms in range(NMS):
                b.wait(t, "p", t_lnT, tgw, t_gv.get(ms - 1))
                for ct in range(CT):
                    inst = t.matmul(ps_lg[:, 0:NEXP], lnT[:, ct, ms * P:(ms + 1) * P],
                                    gw_sb[:, ct, :], start=(ct == 0), stop=(ct == CT - 1))
                tl = b.sig("p", inst)
                b.wait(v, "v", tl, b.at("v"))
                b.sig("v", v.tensor_copy(lgt[:], ps_lg[:, 0:NEXP]))
                b.wait(v, "v", b.at("v"))
                b.sig("v", v.reduce_max(v1[:], lgt[:], axis=AX.X))
                b.wait(v, "v", b.at("v"))
                b.sig("v", v.tensor_scalar(m1[:], lgt[:], v1[:], None, ALU.is_equal))
                b.wait(v, "v", b.at("v"))
                b.sig("v", v.scalar_tensor_tensor(msk[:], m1[:], -1e30, lgt[:], ALU.mult, ALU.add))
                b.wait(v, "v", b.at("v"))
                b.sig("v", v.reduce_max(v2[:], msk[:], axis=AX.X))
                b.wait(v, "v", b.at("v"))
                b.sig("v", v.tensor_scalar(m2[:], msk[:], v2[:], None, ALU.is_equal))
                b.wait(v, "v", b.at("v"))
                tdv = b.sig("v", v.tensor_sub(dv[:], v1[:], v2[:]))
                b.wait(a, "a", tdv, b.at("a"))
                b.sig("a", a.activation(p1[:], dv[:], AF.Sigmoid))
                b.wait(a, "a", b.at("a"))
                tp2 = b.sig("a", a.activation(p2[:], dv[:], AF.Sigmoid, scale=-1.0))
                b.wait(v, "v", tp2, b.at("v"))
                b.sig("v", v.tensor_scalar(gsc[:], m1[:], p1[:], SC2, ALU.mult, ALU.mult))
                b.wait(v, "v", b.at("v"))
                b.sig("v", v.tensor_scalar(m2[:], m2[:], p2[:], SC2, ALU.mult, ALU.mult))
                b.wait(v, "v", b.at("v"))
                b.sig("v", v.tensor_add(gsc[:], gsc[:], m2[:]))
                b.wait(v, "v", b.at("v"))
                t_g = b.sig("v", v.tensor_copy(gall[:, ms, :], gsc[:]))
                b.wait(t, "p", t_lnT, tat, t_g)
                for ct in range(CT):
                    inst = t.matmul(ps_h[:, 0:ER], lnT[:, ct, ms * P:(ms + 1) * P],
                                    at_sb[:, ct, :], start=(ct == 0), stop=(ct == CT - 1))
                th = b.sig("p", inst)
                b.wait(v, "v", th, t_hT.get(ms - 1), b.at("v"))
                th1 = b.sig("v", v.tensor_tensor(
                    h1[:].rearrange("p (e r) -> p e r", r=RANK),
                    ps_h[:, 0:ER].rearrange("p (e r) -> p e r", r=RANK),
                    gall[:, ms, :].unsqueeze(2).to_broadcast((P, NEXP, RANK)), ALU.mult))
                b.wait(t, "p", th1)
                i1 = t.transpose(ps_ht[0][:, 0:P], h1[:, 0:P], ident[:])
                i2 = t.transpose(ps_ht[1][0:ER - P, 0:P], h1[:, P:ER], ident[:])
                tph = b.sig("p", i2)
                b.wait(v, "v", tph, b.at("v"))
                b.sig("v", v.tensor_copy(hT1[:, ms, :], ps_ht[0][:, 0:P]))
                b.wait(v, "v", b.at("v"))
                t_hT[ms] = b.sig("v", v.tensor_copy(hT2[0:ER - P, ms, :], ps_ht[1][0:ER - P, 0:P]))
                t_gv[ms] = t_hT[ms]
            for cc0 in range(16):
                cc = cc0
                gcc = mt * 16 + cc0
                slot = gcc % 2
                b.wait(gp, "dg", t_bwu.get(gcc - 2))
                t_bw[gcc] = b.dma("dg", gp, lambda: gp.dma_start(
                    out=bw[slot][:],
                    in_=aps["baseWT"].rearrange("(ct p) f -> p ct f", p=P)[:, :, cc * 256:(cc + 1) * 256]))
                for ms in range(NMS):
                    k = gcc * NMS + ms
                    b.wait(sy, "ds", t_y.get(k - 2))
                    t_xi[k] = b.dma("ds", sy, lambda: sy.dma_start(
                        out=xi[ms % 2][:],
                        in_=aps["xres"][mt * MT + ms * P:mt * MT + (ms + 1) * P,
                                        cc * 256:(cc + 1) * 256]))
                    b.wait(t, "p", t_bw[gcc], tb1, tb2, t_hT[NMS - 1], t_y.get(k - 2))
                    for ct in range(CT):
                        t.matmul(ps_y[ms % 2][:, 0:256], lnT[:, ct, ms * P:(ms + 1) * P],
                                 bw[slot][:, ct, :], start=(ct == 0), stop=False)
                    t.matmul(ps_y[ms % 2][:, 0:256], hT1[:, ms, :],
                             bb1[:, cc * 256:(cc + 1) * 256], start=False, stop=False)
                    inst = t.matmul(ps_y[ms % 2][:, 0:256], hT2[0:ER - P, ms, :],
                                    bb2[0:ER - P, cc * 256:(cc + 1) * 256],
                                    start=False, stop=True)
                    tm = b.sig("p", inst)
                    b.wait(v, "v", tm, t_xi[k], t_xw.get(k - 2), b.at("v"))
                    t_y[k] = b.sig("v", v.tensor_tensor(
                        xn[ms % 2][:], ps_y[ms % 2][:, 0:256], xi[ms % 2][:], ALU.add))
                    b.wait(sy, "ds", t_y[k])
                    t_xw[k] = b.dma("ds", sy, lambda: sy.dma_start(
                        out=aps["xdst"](mt * NMS + ms)[:, cc * 256:(cc + 1) * 256],
                        in_=xn[ms % 2][:]))
                t_bwu[gcc] = b.at("p")
                b.pump()

    b.fence([(sy, "ds"), (gp, "dg"), (v, "v"), (a, "a"), (t, "p")])


def build_ffn1_program():
    nc = bass.Bass("TRN2", target_bir_lowering=False)
    xin = nc.dram_tensor("xin", [OWN, D], F32, kind="ExternalInput").ap()
    baseWT = nc.dram_tensor("baseWT", [D, D], F32, kind="ExternalInput").ap()
    aallT = nc.dram_tensor("aallT", [D, ER], F32, kind="ExternalInput").ap()
    ballT = nc.dram_tensor("ballT", [ER, D], F32, kind="ExternalInput").ap()
    gwT = nc.dram_tensor("gwT", [D, NEXP], F32, kind="ExternalInput").ap()
    ident = nc.dram_tensor("ident", [P, P], BF16, kind="ExternalInput").ap()
    xout = nc.dram_tensor("xout", [OWN, D], F32, kind="ExternalOutput").ap()

    with ExitStack() as es:
        b = Bld(nc, es)
        es.enter_context(nc.Block())
        cons = emit_consts(b, ident)
        _CONS_TG[id(b)] = cons["t_g"]
        aps = {
            "xsrc": xin, "xres": xin, "baseWT": baseWT, "aallT": aallT,
            "ballT": ballT, "gwT": gwT,
            "xdst": lambda i: xout[i * P:(i + 1) * P, :],
        }
        emit_ffn1(b, "f1", aps, cons, {})
    return nc


def stage_ffn1_inputs(x_own_per_core, base_W, gate_W, lora_A, lora_B):
    ident = np.eye(P, dtype=ml_dtypes.bfloat16)
    baseWT = np.ascontiguousarray(np.asarray(base_W, np.float32).T)
    aallT = np.ascontiguousarray(np.asarray(lora_A, np.float32).transpose(2, 0, 1).reshape(D, ER))
    ballT = np.ascontiguousarray(np.asarray(lora_B, np.float32).transpose(0, 2, 1).reshape(ER, D))
    gwT = np.ascontiguousarray(np.asarray(gate_W, np.float32).T)
    maps = []
    for c in range(NCORE):
        maps.append({
            "xin": np.ascontiguousarray(x_own_per_core[c]),
            "baseWT": baseWT, "aallT": aallT, "ballT": ballT, "gwT": gwT,
            "ident": ident,
        })
    return maps





def _chunks(total, k):
    out = []
    o = 0
    while o < total:
        w = min(k, total - o)
        out.append((o, w))
        o += w
    return out


def emit_unpack(b, tag, bufs, q_ap, ncols, wf):
    nc = b.nc
    v = nc.vector
    qt, sh, an = bufs["qt"], bufs["sh"], bufs["an"]
    rows = wf.shape[0]
    for i in range(8):
        if i == 0:
            src_i = qt
        else:
            b.wait(v, "v", b.at("v"))
            b.sig("v", v.tensor_single_scalar(sh[0:rows, 0:ncols], qt[0:rows, 0:ncols],
                                              4 * i, ALU.logical_shift_right))
            src_i = sh
        b.wait(v, "v", b.at("v"))
        b.sig("v", v.tensor_single_scalar(an[0:rows, 0:ncols], src_i[0:rows, 0:ncols],
                                          15, ALU.bitwise_and))
        b.wait(v, "v", b.at("v"))
        b.sig("v", v.tensor_copy(
            wf[0:rows, :].rearrange("p (k e) -> p k e", e=8)[:, :, i:i + 1].squeeze(2),
            an[0:rows, 0:ncols]))
    return b.at("v")


def emit_zs_prep(b, tag, aps, cons, zs_dram):
    nc = b.nc
    sy, gp, v = nc.sync, nc.gpsimd, nc.vector
    with ExitStack() as ph:
        Sb = lambda n, s, d: ph.enter_context(nc.sbuf_tensor(tag + n, s, d))
        bufs = {
            "qt": Sb("_qt", [NG_D, I_FFN // 8], I32),
            "sh": Sb("_sh", [NG_D, I_FFN // 8], I32),
            "an": Sb("_an", [NG_D, I_FFN // 8], I32),
        }
        zf = Sb("_zf", [NG_D, I_FFN], F32)
        sf = Sb("_sf", [NG_D, I_FFN], F32)
        zb = Sb("_zb", [NG_D, I_FFN], BF16)
        t_zdma = None
        for name, qz_ap, s_ap, ng, ncol, dst in (
            ("g", aps["gate_qz"], aps["gate_s"], NG_GU, I_FFN, zs_dram["g"]),
            ("u", aps["up_qz"], aps["up_s"], NG_GU, I_FFN, zs_dram["u"]),
            ("d", aps["down_qz"], aps["down_s"], NG_D, D, zs_dram["d"]),
        ):
            b.wait(sy, "ds", b.at("v"))
            tq = b.dma("ds", sy, lambda: sy.dma_start(out=bufs["qt"][0:ng, 0:ncol // 8], in_=qz_ap))
            tsv = b.dma("ds", sy, lambda: sy.dma_start(out=sf[0:ng, 0:ncol], in_=s_ap))
            b.wait(v, "v", tq)
            emit_unpack(b, tag + name, bufs, qz_ap, ncol // 8, zf[0:ng, 0:ncol])
            b.wait(v, "v", tsv, t_zdma, b.at("v"))
            tz = b.sig("v", v.scalar_tensor_tensor(zb[0:ng, 0:ncol], zf[0:ng, 0:ncol],
                                                   -1.0, sf[0:ng, 0:ncol], ALU.mult, ALU.mult))
            b.wait(sy, "ds", tz)
            t_zdma = b.dma("ds", sy, lambda: sy.dma_start(out=dst, in_=zb[0:ng, 0:ncol]))
        b.fence([(nc.sync, "ds"), (nc.gpsimd, "dg"), (nc.vector, "v"), (nc.tensor, "p")])
    return b.at("ds")


def emit_dequant_half(b, tag, aps, cons, mine, cw=IHALF):
    nc = b.nc
    sy, gp, v, t = nc.sync, nc.gpsimd, nc.vector, nc.tensor
    ones_b = cons["ones_b"]
    with ExitStack() as ph:
        Sb = lambda n, s, d: ph.enter_context(nc.sbuf_tensor(tag + n, s, d))
        Ps = lambda n, s, d: ph.enter_context(nc.psum_tensor(tag + n, s, d))
        bufs = {
            "qt": Sb("_qt", [P, cw // 8], I32),
            "sh": Sb("_sh", [P, cw // 8], I32),
            "an": Sb("_an", [P, cw // 8], I32),
        }
        wf = Sb("_wf", [P, cw], F32)
        wb = [Sb(f"_wb{i}", [P, cw], BF16) for i in range(2)]
        srow = [Sb(f"_srow{i}", [1, cw], BF16) for i in range(2)]
        ps_s = [Ps(f"_pss{i}", [P, 512], F32) for i in range(2)]

        t_dma = {}
        t_sused = {}
        gi = 0
        for mat, qw_ap, s_ap, ng, htot, dst in (
            ("g", aps["gate_qw"], aps["gate_s_h"], NG_GU, IHALF, mine["g"]),
            ("u", aps["up_qw"], aps["up_s_h"], NG_GU, IHALF, mine["u"]),
            ("d", aps["down_qw"], aps["down_s_h"], NG_D, CHALF, mine["d"]),
        ):
            for g in range(ng):
                for c0, cwi in _chunks(htot, cw):
                    slot = gi % 2
                    b.wait(sy, "ds", b.at("v"))
                    tq = b.dma("ds", sy, lambda: sy.dma_start(
                        out=bufs["qt"][:, 0:cwi // 8],
                        in_=qw_ap[g * P:(g + 1) * P, c0 // 8:(c0 + cwi) // 8]))
                    b.wait(gp, "dg", t_sused.get(gi - 2))
                    tsr = b.dma("dg", gp, lambda: gp.dma_start(
                        out=srow[gi % 2][:, 0:cwi], in_=s_ap[g:g + 1, c0:c0 + cwi]))
                    b.wait(v, "v", tq)
                    emit_unpack(b, tag + mat, bufs, qw_ap, cwi // 8, wf[:, 0:cwi])
                    tun = b.at("v")
                    tlast = None
                    for k, (o2, w2) in enumerate(_chunks(cwi, 512)):
                        b.wait(t, "p", tsr, cons["t_g"], b.at("v"))
                        tmm = b.sig("p", t.matmul(ps_s[k % 2][:, 0:w2], ones_b[:],
                                                  srow[gi % 2][0:1, o2:o2 + w2],
                                                  start=True, stop=True))
                        b.wait(v, "v", tmm, tun, t_dma.get(gi - 2), b.at("v"))
                        tlast = b.sig("v", v.tensor_tensor(
                            wb[slot][:, o2:o2 + w2], wf[:, o2:o2 + w2],
                            ps_s[k % 2][:, 0:w2], ALU.mult))
                    t_sused[gi] = b.at("p")
                    b.wait(sy, "ds", tlast)
                    t_dma[gi] = b.dma("ds", sy, lambda: sy.dma_start(
                        out=dst[g * P:(g + 1) * P, c0:c0 + cwi], in_=wb[slot][:, 0:cwi]))
                    gi += 1
                    yield None
        b.fence([(sy, "ds"), (gp, "dg"), (v, "v"), (t, "p")])
    yield None


_BAR_N = {}


def emit_share_and_barrier(b, tag, mine, sh, shf_ap, phase, extra_wait=None, solo=False):
    nc = b.nc
    sy, gp, v = nc.sync, nc.gpsimd, nc.vector
    es = b.es
    n = _BAR_N.get(id(b), 0)
    _BAR_N[id(b)] = n + 1
    sps = b.sem(f"sps{n}")
    tokp = es.enter_context(nc.sbuf_tensor(tag + "_tokp", [1, 16], I32))
    f0 = es.enter_context(nc.sbuf_tensor(tag + "_f0", [1, 16], I32))
    f1 = es.enter_context(nc.sbuf_tensor(tag + "_f1", [1, 16], I32))
    b.wait(v, "v", cons_tok(b), b.at("v"))
    ttk = b.sig("v", v.tensor_scalar_add(tokp[:], tok_sb_of(b)[:], phase))
    t_cp = [None]

    def _copies(half):
        tickets = []
        if mine is not None:
            halves = (half, 1 - half) if solo else (half,)
            for key in ("g", "u", "d"):
                src = mine[key]
                cols = src.shape[1]
                for hh_ in halves:
                    dstv = sh[key][:, hh_ * cols:(hh_ + 1) * cols]
                    tickets.append(b.dma("ds", sy, lambda: sy.dma_start(out=dstv, in_=src)))
        if extra_wait is not None:
            b.wait(gp, "dg", extra_wait)
        for tk_ in tickets:
            b.wait(gp, "dg", tk_)
        b.wait(gp, "dg", ttk)
        if solo:
            tc = b.dma("dg", gp, lambda: gp.dma_start(out=shf_ap[0], in_=tokp[:]))
            b.wait(gp, "dg", tc)
            t_cp[0] = b.dma("dg", gp, lambda: gp.dma_start(out=shf_ap[1], in_=tokp[:]))
        else:
            t_cp[0] = b.dma("dg", gp, lambda: gp.dma_start(out=shf_ap[half], in_=tokp[:]))

    b.switched([sy, gp], _copies)
    b.wait(gp, "dg", t_cp[0])
    with gp.register("exp") as exp, gp.register("t0") as t0, gp.register("t1") as t1, \
         gp.register("w0") as w0, gp.register("cmpr") as cmpr, gp.register("c2") as c2:
        gp.reg_load(t0, tokp[0:1, 0:1])
        gp.reg_load(t1, tokp[0:1, 1:2])
        gp.reg_mov(exp, 32)

        def cond():
            gp.dma_start(out=f0[:], in_=shf_ap[0]).then_inc(sps, 16)
            gp.dma_start(out=f1[:], in_=shf_ap[1]).then_inc(sps, 16)
            gp.wait_ge(sps, exp)
            gp.reg_add(exp, exp, 32)
            gp.reg_load(w0, f0[0:1, 0:1])
            gp.reg_alu(cmpr, w0, t0, ALU.not_equal)
            gp.reg_load(w0, f0[0:1, 1:2])
            gp.reg_alu(c2, w0, t1, ALU.not_equal)
            gp.reg_alu(cmpr, cmpr, c2, ALU.bitwise_or)
            gp.reg_load(w0, f1[0:1, 0:1])
            gp.reg_alu(c2, w0, t0, ALU.not_equal)
            gp.reg_alu(cmpr, cmpr, c2, ALU.bitwise_or)
            gp.reg_load(w0, f1[0:1, 1:2])
            gp.reg_alu(c2, w0, t1, ALU.not_equal)
            gp.reg_alu(cmpr, cmpr, c2, ALU.bitwise_or)
            return cmpr

        with gp.While(cond):
            gp.nop()
    return b.sig("g", gp.nop())


_TOK = {}


def tok_sb_of(b):
    return _TOK[id(b)][0]


def cons_tok(b):
    return _TOK[id(b)][1]


def emit_ffn0(b, tag, aps, cons, wsh, zs_dram, t_bar):
    nc = b.nc
    sy, gp, v, a, t = nc.sync, nc.gpsimd, nc.vector, nc.scalar, nc.tensor
    MT = 512
    NMT = OWN // MT
    NMS = MT // P
    SEGS = [(0, 22), (22, 22), (44, 21), (65, 21)]
    MAXSEG = 22

    with ExitStack() as ph:
        Sb = lambda n, s, d: ph.enter_context(nc.sbuf_tensor(tag + n, s, d))
        Ps = lambda n, s, d: ph.enter_context(nc.psum_tensor(tag + n, s, d))
        lnT = Sb("_lnT", [P, CT, MT], BF16)
        lbufs = mk_ln_bufs(b, tag, ph, cons, nlnb=2, sink=False, nxd=1)
        hT = Sb("_hT", [P, MAXSEG, MT], BF16)
        wgb = [Sb(f"_wgb{i}", [P, CT, 256], BF16) for i in range(2)]
        wub = [Sb(f"_wub{i}", [P, CT, 256], BF16) for i in range(2)]
        wdb = [Sb(f"_wdb{i}", [P, 8, 512], BF16) for i in range(2)]
        zsg = Sb("_zsg", [NG_GU, MAXSEG * P], BF16)
        zsu = Sb("_zsu", [NG_GU, MAXSEG * P], BF16)
        zsd = Sb("_zsd", [MAXSEG, D], BF16)
        xgF = Sb("_xgF", [NG_GU, MT], F32)
        hgF = xgF
        xgT = Sb("_xgT", [NG_GU, MT], BF16)
        xgL = Sb("_xgL", [NG_GU, MT], BF16)
        hgT = Sb("_hgT", [MAXSEG, MT], BF16)
        hgL = Sb("_hgL", [MAXSEG, MT], BF16)
        sil = [Sb(f"_sil{i}", [P, MT], BF16) for i in range(2)]
        ysp = [Sb(f"_ysp{i}", [P, 512], F32) for i in range(2)]
        xi = [Sb(f"_xi{i}", [P, 512], F32) for i in range(2)]
        xn = [Sb(f"_xn{i}", [P, 512], F32) for i in range(2)]
        ps = [Ps(f"_ps{i}", [P, 512], F32) for i in range(4)]

        yd = aps["y_spill"]

        t_h = {}
        t_wgu = {}
        t_wdu = {}
        t_dn = {}
        t_ysp_w = {}
        t_ysp_rd = {}
        t_ev = {}
        t_out = {}
        gj = [0]
        gb = [0]
        ge = [0]

        for mt in range(NMT):
            t_lnT = emit_ln_chunk(b, tag, cons, lbufs, aps["xsrc"], mt * MT, NMS, lnT,
                                  b.at("p"))
            b.wait(t, "p", t_lnT, cons["t_g"], b.at("v"))
            for ct in range(CT):
                inst = t.matmul(ps[0][0:NG_GU, :], oh_lhsT(cons, NG_GU, ct),
                                lnT[:, ct, :], start=(ct == 0), stop=(ct == CT - 1))
            txg = b.sig("p", inst)
            b.wait(v, "v", txg, b.at("v"))
            b.sig("v", v.tensor_copy(xgF[:], ps[0][0:NG_GU, :]))
            b.wait(v, "v", b.at("v"))
            b.sig("v", v.tensor_copy(xgT[:], xgF[:]))
            b.wait(v, "v", b.at("v"))
            b.sig("v", v.tensor_tensor(xgF[:], xgF[:], xgT[:], ALU.subtract))
            b.wait(v, "v", b.at("v"))
            txge = b.sig("v", v.tensor_copy(xgL[:], xgF[:]))

            for si, (s0, ns) in enumerate(SEGS):
                b.wait(gp, "dg", b.at("p"))
                tzg = b.dma("dg", gp, lambda: gp.dma_start(
                    out=zsg[:, 0:ns * P],
                    in_=zs_dram["g"][:, s0 * P:(s0 + ns) * P]))
                tzu = b.dma("dg", gp, lambda: gp.dma_start(
                    out=zsu[:, 0:ns * P],
                    in_=zs_dram["u"][:, s0 * P:(s0 + ns) * P]))
                tzd = b.dma("dg", gp, lambda: gp.dma_start(
                    out=zsd[0:ns, :], in_=zs_dram["d"][s0:s0 + ns, :]))
                b.wait(t, "p", b.at("v"))
                t_prev_down = b.at("p")
                for itp in range(0, ns, 2):
                    nit2 = min(2, ns - itp)
                    slot = (gj[0] // 2) % 2
                    b.wait(gp, "dg", t_wgu.get(gj[0] - 3), t_bar)
                    twg = b.dma("dg", gp, lambda: gp.dma_start(
                        out=wgb[slot][:, :, 0:nit2 * P],
                        in_=wsh["g"].rearrange("(ct p) i -> p ct i", p=P)
                        [:, :, (s0 + itp) * P:(s0 + itp + nit2) * P]))
                    twu = b.dma("dg", gp, lambda: gp.dma_start(
                        out=wub[slot][:, :, 0:nit2 * P],
                        in_=wsh["u"].rearrange("(ct p) i -> p ct i", p=P)
                        [:, :, (s0 + itp) * P:(s0 + itp + nit2) * P]))
                    for j in range(nit2):
                        it = itp + j
                        g = gj[0]
                        pg, pu = g % 2, 2 + g % 2
                        b.wait(t, "p", twg, twu, t_bar, tzg, tzu, txge, t_h.get(g - 2))
                        for ct in range(CT):
                            t.matmul(ps[pg][:], wgb[slot][:, ct, j * P:(j + 1) * P],
                                     lnT[:, ct, :], start=(ct == 0), stop=False)
                        t.matmul(ps[pg][:], zsg[:, it * P:(it + 1) * P],
                                 xgT[:], start=False, stop=False)
                        t.matmul(ps[pg][:], zsg[:, it * P:(it + 1) * P],
                                 xgL[:], start=False, stop=True)
                        for ct in range(CT):
                            t.matmul(ps[pu][:], wub[slot][:, ct, j * P:(j + 1) * P],
                                     lnT[:, ct, :], start=(ct == 0), stop=False)
                        t.matmul(ps[pu][:], zsu[:, it * P:(it + 1) * P],
                                 xgT[:], start=False, stop=False)
                        tum = b.sig("p", t.matmul(
                            ps[pu][:], zsu[:, it * P:(it + 1) * P],
                            xgL[:], start=False, stop=True))
                        b.wait(a, "a", tum, b.at("a"), t_h.get(g - 2))
                        tsl = b.sig("a", a.activation(sil[g % 2][:], ps[pg][:], AF.Silu))
                        b.wait(v, "v", tsl, t_prev_down, b.at("v"))
                        t_h[g] = b.sig("v", v.tensor_tensor(
                            hT[:, it, :], sil[g % 2][:], ps[pu][:], ALU.mult))
                        gj[0] += 1
                        t_wgu[gj[0] - 1] = b.at("p")
                    b.pump()
                t_hlast = b.at("v")
                b.wait(t, "p", t_hlast)
                for it in range(ns):
                    inst = t.matmul(ps[0][0:ns, :], oh_lhsT(cons, ns, it),
                                    hT[:, it, :], start=(it == 0), stop=(it == ns - 1))
                thg = b.sig("p", inst)
                b.wait(v, "v", thg, b.at("v"))
                b.sig("v", v.tensor_copy(hgF[0:ns, :], ps[0][0:ns, :]))
                b.wait(v, "v", b.at("v"))
                b.sig("v", v.tensor_copy(hgT[0:ns, :], hgF[0:ns, :]))
                b.wait(v, "v", b.at("v"))
                b.sig("v", v.tensor_tensor(hgF[0:ns, :], hgF[0:ns, :], hgT[0:ns, :], ALU.subtract))
                b.wait(v, "v", b.at("v"))
                thge = b.sig("v", v.tensor_copy(hgL[0:ns, :], hgF[0:ns, :]))
                nblk = _chunks(ns, 8)
                for cc in range(8):
                    for bi, (it0, bw_) in enumerate(nblk):
                        gbd = gb[0]
                        gb[0] += 1
                        slot = gbd % 2
                        b.wait(sy, "ds", t_wdu.get(gbd - 1), t_bar)
                        twd = b.dma("ds", sy, lambda: sy.dma_start(
                            out=wdb[slot][:, 0:bw_, :],
                            in_=wsh["d"][(s0 + it0) * P:(s0 + it0 + bw_) * P,
                                         cc * 512:(cc + 1) * 512]
                            .rearrange("(k p) c -> p k c", p=P)))
                        for ms in range(NMS):
                            pd = ms
                            b.wait(t, "p", twd, thge, tzd,
                                   t_dn.get((mt, si, cc - 1, ms, "ev")))
                            for j in range(bw_):
                                it = it0 + j
                                inst = t.matmul(ps[pd][:], hT[:, it, ms * P:(ms + 1) * P],
                                                wdb[slot][:, j, :],
                                                start=(bi == 0 and j == 0), stop=False)
                            if bi == len(nblk) - 1:
                                t.matmul(ps[pd][:], hgT[0:ns, ms * P:(ms + 1) * P],
                                         zsd[0:ns, cc * 512:(cc + 1) * 512],
                                         start=False, stop=False)
                                inst = t.matmul(ps[pd][:], hgL[0:ns, ms * P:(ms + 1) * P],
                                                zsd[0:ns, cc * 512:(cc + 1) * 512],
                                                start=False, stop=True)
                                tmm = b.sig("p", inst)
                                t_dn[(mt, si, cc, ms)] = tmm
                                k = ge[0]
                                ge[0] += 1
                                if si == 0:
                                    b.wait(v, "v", tmm, t_ysp_rd.get((mt - 1, 3, cc, ms)),
                                           t_ysp_w.get(k - 2), b.at("v"))
                                    tev = b.sig("v", v.tensor_copy(ysp[k % 2][:], ps[pd][:]))
                                    t_dn[(mt, si, cc, ms, "ev")] = tev
                                    b.wait(sy, "ds", tev)
                                    t_ysp_w[k] = b.dma("ds", sy, lambda: sy.dma_start(
                                        out=yd[ms * P:(ms + 1) * P, cc * 512:(cc + 1) * 512],
                                        in_=ysp[k % 2][:]))
                                    t_ysp_w[(mt, si, cc, ms)] = t_ysp_w[k]
                                elif si < 3:
                                    b.wait(sy, "ds", t_ysp_w[(mt, si - 1, cc, ms)],
                                           t_ysp_w.get(k - 2), t_ev.get(k - 2))
                                    trd = b.dma("ds", sy, lambda: sy.dma_start(
                                        out=ysp[k % 2][:],
                                        in_=yd[ms * P:(ms + 1) * P, cc * 512:(cc + 1) * 512]))
                                    t_ysp_rd[(mt, si, cc, ms)] = trd
                                    b.wait(v, "v", tmm, trd, t_ev.get(k - 2), b.at("v"))
                                    tev = b.sig("v", v.tensor_tensor(
                                        ysp[k % 2][:], ps[pd][:], ysp[k % 2][:], ALU.add))
                                    t_ev[k] = tev
                                    t_dn[(mt, si, cc, ms, "ev")] = tev
                                    b.wait(sy, "ds", tev)
                                    t_ysp_w[k] = b.dma("ds", sy, lambda: sy.dma_start(
                                        out=yd[ms * P:(ms + 1) * P, cc * 512:(cc + 1) * 512],
                                        in_=ysp[k % 2][:]))
                                    t_ysp_w[(mt, si, cc, ms)] = t_ysp_w[k]
                                else:
                                    b.wait(sy, "ds", t_ysp_w[(mt, si - 1, cc, ms)],
                                           t_ysp_w.get(k - 2), t_ev.get(k - 2),
                                           t_out.get(k - 2))
                                    trd = b.dma("ds", sy, lambda: sy.dma_start(
                                        out=ysp[k % 2][:],
                                        in_=yd[ms * P:(ms + 1) * P, cc * 512:(cc + 1) * 512]))
                                    t_ysp_rd[(mt, si, cc, ms)] = trd
                                    txi = b.dma("ds", sy, lambda: sy.dma_start(
                                        out=xi[k % 2][:],
                                        in_=aps["xres"][mt * MT + ms * P:mt * MT + (ms + 1) * P,
                                                        cc * 512:(cc + 1) * 512]))
                                    b.wait(v, "v", tmm, trd, txi, t_out.get(k - 2),
                                           t_ev.get(k - 2), b.at("v"))
                                    t_dn[(mt, si, cc, ms, "ev")] = b.sig(
                                        "v", v.tensor_tensor(
                                            xn[k % 2][:], ps[pd][:], ysp[k % 2][:], ALU.add))
                                    b.wait(v, "v", b.at("v"))
                                    t_ev[k] = b.sig("v", v.tensor_tensor(
                                        xn[k % 2][:], xn[k % 2][:], xi[k % 2][:], ALU.add))
                                    b.wait(sy, "ds", t_ev[k])
                                    t_out[k] = b.dma("ds", sy, lambda: sy.dma_start(
                                        out=aps["xdst"](mt * NMS + ms)[:, cc * 512:(cc + 1) * 512],
                                        in_=xn[k % 2][:]))
                        if bi < len(nblk) - 1:
                            b.sig("p", inst)
                        t_wdu[gbd] = b.at("p")
                    b.pump()

    b.fence([(sy, "ds"), (gp, "dg"), (v, "v"), (a, "a"), (t, "p"), (gp, "g")])


def build_ffn0_program(solo=False):
    nc = bass.Bass("TRN2", target_bir_lowering=False)
    spc = dict() if solo else dict(addr_space="Shared")
    xin = nc.dram_tensor("xin", [OWN, D], F32, kind="ExternalInput").ap()
    gate_qw = nc.dram_tensor("gate_qw", [D, IHALF // 8], I32, kind="ExternalInput").ap()
    up_qw = nc.dram_tensor("up_qw", [D, IHALF // 8], I32, kind="ExternalInput").ap()
    down_qw = nc.dram_tensor("down_qw", [I_FFN, CHALF // 8], I32, kind="ExternalInput").ap()
    gate_s = nc.dram_tensor("gate_s", [NG_GU, I_FFN], F32, kind="ExternalInput").ap()
    up_s = nc.dram_tensor("up_s", [NG_GU, I_FFN], F32, kind="ExternalInput").ap()
    down_s = nc.dram_tensor("down_s", [NG_D, D], F32, kind="ExternalInput").ap()
    gate_qz = nc.dram_tensor("gate_qz", [NG_GU, I_FFN // 8], I32, kind="ExternalInput").ap()
    up_qz = nc.dram_tensor("up_qz", [NG_GU, I_FFN // 8], I32, kind="ExternalInput").ap()
    down_qz = nc.dram_tensor("down_qz", [NG_D, D // 8], I32, kind="ExternalInput").ap()
    gate_s_h = nc.dram_tensor("gate_s_h", [NG_GU, IHALF], F32, kind="ExternalInput").ap()
    up_s_h = nc.dram_tensor("up_s_h", [NG_GU, IHALF], F32, kind="ExternalInput").ap()
    down_s_h = nc.dram_tensor("down_s_h", [NG_D, CHALF], F32, kind="ExternalInput").ap()
    tok = nc.dram_tensor("tok", [1, 16], I32, kind="ExternalInput").ap()
    ident = nc.dram_tensor("ident", [P, P], BF16, kind="ExternalInput").ap()
    xout = nc.dram_tensor("xout", [OWN, D], F32, kind="ExternalOutput").ap()

    wg_m = nc.dram_tensor("wg_m", [D, IHALF], BF16, kind="Internal").ap()
    wu_m = nc.dram_tensor("wu_m", [D, IHALF], BF16, kind="Internal").ap()
    wd_m = nc.dram_tensor("wd_m", [I_FFN, CHALF], BF16, kind="Internal").ap()
    wg_sh = nc.dram_tensor("wg_sh", [D, I_FFN], BF16, kind="Internal", **spc).ap()
    wu_sh = nc.dram_tensor("wu_sh", [D, I_FFN], BF16, kind="Internal", **spc).ap()
    wd_sh = nc.dram_tensor("wd_sh", [I_FFN, D], BF16, kind="Internal", **spc).ap()
    shf = nc.dram_tensor("shf", [2, 1, 16], I32, kind="Internal", **spc).ap()
    zs_g = nc.dram_tensor("zs_g", [NG_GU, I_FFN], BF16, kind="Internal").ap()
    zs_u = nc.dram_tensor("zs_u", [NG_GU, I_FFN], BF16, kind="Internal").ap()
    zs_d = nc.dram_tensor("zs_d", [NG_D, D], BF16, kind="Internal").ap()
    y_sp = nc.dram_tensor("y_sp", [512, D], F32, kind="Internal").ap()
    dbg = nc.dram_tensor("dbg", [4 * P, 512], F32, kind="Internal").ap()

    with ExitStack() as es:
        b = Bld(nc, es)
        es.enter_context(nc.Block())
        cons = emit_consts(b, ident)
        _CONS_TG[id(b)] = cons["t_g"]
        tok_sb = es.enter_context(nc.sbuf_tensor("c_tok", [1, 16], I32))
        ttok = b.dma("ds", nc.sync, lambda: nc.sync.dma_start(out=tok_sb[:], in_=tok))
        _TOK[id(b)] = (tok_sb, ttok)

        aps = {
            "gate_qw": gate_qw, "up_qw": up_qw, "down_qw": down_qw,
            "gate_s": gate_s, "up_s": up_s, "down_s": down_s,
            "gate_qz": gate_qz, "up_qz": up_qz, "down_qz": down_qz,
            "gate_s_h": gate_s_h, "up_s_h": up_s_h, "down_s_h": down_s_h,
            "xsrc": xin, "xres": xin,
            "xdst": lambda i: xout[i * P:(i + 1) * P, :],
            "y_spill": y_sp,
        }
        zs_dram = {"g": zs_g, "u": zs_u, "d": zs_d}
        mine = {"g": wg_m, "u": wu_m, "d": wd_m}
        wsh = {"g": wg_sh, "u": wu_sh, "d": wd_sh}

        emit_zs_prep(b, "zs", aps, cons, zs_dram)
        for _ in emit_dequant_half(b, "dq", aps, cons, mine):
            pass
        t_bar = emit_share_and_barrier(b, "bar", mine, wsh, shf, 0, solo=solo)
        emit_ffn0(b, "f0", aps, cons, wsh, zs_dram, t_bar)
    return nc


def stage_ffn0_inputs(x_own_per_core, gq, gs, gz, uq, us, uz, dq, ds_, dz, tok_val=None):
    ident = np.eye(P, dtype=ml_dtypes.bfloat16)
    if tok_val is None:
        tok_val = np.random.randint(1 << 20, 1 << 30, size=2, dtype=np.int32)
    tok = np.zeros((1, 16), np.int32)
    tok[0, 0] = tok_val[0]
    tok[0, 1] = tok_val[1]
    maps = []
    for c in range(NCORE):
        half = c % 2
        i0, i1 = half * IHALF, (half + 1) * IHALF
        c0, c1 = half * CHALF, (half + 1) * CHALF
        maps.append({
            "xin": np.ascontiguousarray(x_own_per_core[c]),
            "gate_qw": np.ascontiguousarray(gq[:, i0 // 8:i1 // 8]),
            "up_qw": np.ascontiguousarray(uq[:, i0 // 8:i1 // 8]),
            "down_qw": np.ascontiguousarray(dq[:, c0 // 8:c1 // 8]),
            "gate_s": np.ascontiguousarray(gs),
            "up_s": np.ascontiguousarray(us),
            "down_s": np.ascontiguousarray(ds_),
            "gate_qz": np.ascontiguousarray(gz),
            "up_qz": np.ascontiguousarray(uz),
            "down_qz": np.ascontiguousarray(dz),
            "gate_s_h": np.ascontiguousarray(gs[:, i0:i1]),
            "up_s_h": np.ascontiguousarray(us[:, i0:i1]),
            "down_s_h": np.ascontiguousarray(ds_[:, c0:c1]),
            "tok": tok,
            "ident": ident,
        })
    return maps


_PROGS = {}


def _prog(name, fn, *a, **k):
    key = (name, a, tuple(sorted(k.items())))
    if key not in _PROGS:
        _PROGS[key] = fn(*a, **k)
    return _PROGS[key]


def kernel(**inputs):
    ii = {k: np.asarray(v) for k, v in inputs.items()}
    x0 = ii["L0_attn_x"].astype(np.float32)

    cores = list(range(NCORE))

    nc_a = _prog("attn", build_attn_program)
    maps = stage_attn_inputs(x0, ii["L0_attn_Wqkv"], ii["L0_attn_bqkv"],
                             ii["L0_attn_Wo"], ii["L0_attn_bo"])
    res = run_bass_kernel_spmd(nc_a, maps, core_ids=cores)
    x1_own = [res.results[c]["xout"] for c in cores]

    nc_f0 = _prog("ffn0", build_ffn0_program)
    maps = stage_ffn0_inputs(
        x1_own,
        ii["L0_ffn_gate_qweight"], ii["L0_ffn_gate_scales"], ii["L0_ffn_gate_qzeros"],
        ii["L0_ffn_up_qweight"], ii["L0_ffn_up_scales"], ii["L0_ffn_up_qzeros"],
        ii["L0_ffn_down_qweight"], ii["L0_ffn_down_scales"], ii["L0_ffn_down_qzeros"])
    res = run_bass_kernel_spmd(nc_f0, maps, core_ids=cores)
    x2_own = [res.results[c]["xout"] for c in cores]

    x2 = np.empty((B_, S, D), np.float32)
    for c in cores:
        x2[c // 2, (c % 2) * OWN:(c % 2 + 1) * OWN] = x2_own[c]
    maps = stage_attn_inputs(x2, ii["L1_attn_Wqkv"], ii["L1_attn_bqkv"],
                             ii["L1_attn_Wo"], ii["L1_attn_bo"])
    res = run_bass_kernel_spmd(nc_a, maps, core_ids=cores)
    x3_own = [res.results[c]["xout"] for c in cores]

    nc_f1 = _prog("ffn1", build_ffn1_program)
    maps = stage_ffn1_inputs(x3_own, ii["L1_ffn_base_W"], ii["L1_ffn_gate_W"],
                             ii["L1_ffn_lora_A"], ii["L1_ffn_lora_B"])
    res = run_bass_kernel_spmd(nc_f1, maps, core_ids=cores)

    out = np.empty((B_, S, D), np.float32)
    for c in cores:
        out[c // 2, (c % 2) * OWN:(c % 2 + 1) * OWN] = res.results[c]["xout"]
    return out
